# version 01 (final)
# baseline (speedup 1.0000x reference)
import math
from contextlib import ExitStack
from types import SimpleNamespace

import numpy as np
import ml_dtypes

import concourse.bass as bass
import concourse.mybir as mybir
from concourse.bass_utils import run_bass_kernel_spmd
from concourse.ap import AP

F32 = mybir.dt.float32
BF16 = mybir.dt.bfloat16
U8 = mybir.dt.uint8
AF = mybir.ActivationFunctionType
ALU = mybir.AluOpType

B, SQ, D = 4, 4096, 4096
H, PDIM, GRP, NS = 128, 64, 8, 128
KC, CHK = 4, 128
DI, CONVD, PROJD = 8192, 10240, 18560
T = 2048
LEAD = 128
TX = T + LEAD
NCH = T // CHK
NDC = D // 128
EPS = 1e-5
R01 = 40
CLAMP = -30.0

def ntiles(total, w=512):
    out = []
    o = 0
    while o < total:
        out.append((o, min(w, total - o)))
        o += w
    return out

NT_TX = ntiles(TX)
NT_T = ntiles(T)

DEBUG_PHASES = 99
DEBUG_TAPS = ()
DEBUG_NO_CC = False

def _fourier_q():
    n = D
    t = np.arange(n)
    q = np.empty((n, n), dtype=np.float64)
    q[0, :] = 1.0 / math.sqrt(n)
    f = np.arange(1, (n - 1) // 2 + 1)
    ang = 2.0 * np.pi * np.outer(f, t) / n
    q[1:1 + 2 * len(f):2, :] = math.sqrt(2.0 / n) * np.cos(ang)
    q[2:2 + 2 * len(f):2, :] = math.sqrt(2.0 / n) * np.sin(ang)
    q[n - 1, :] = ((-1.0) ** t) / math.sqrt(n)
    return q.astype(np.float32)

class Sem:
    __slots__ = ("h", "n", "step", "name")
    def __init__(self, nc, stack, name, step=1):
        self.h = stack.enter_context(nc.semaphore(name))
        self.n = 0
        self.step = step
        self.name = name
    def tok(self):
        return (self, self.n)

ENGS = ("sy", "ge", "te", "ve", "se")

class Prog:
    def __init__(self, nc, stack):
        self.nc = nc
        self.stack = stack
        self.q = {e: [] for e in ENGS}
        self.wmax = {}
        self.sems = []
    def sem(self, name, step=1):
        s = Sem(self.nc, self.stack, name, step)
        self.sems.append(s)
        return s
    def emit(self, eng, build, waits=(), inc=None):
        ws = []
        for w in waits:
            s, v = (w, w.n) if isinstance(w, Sem) else w
            if v <= 0:
                continue
            key = (eng, s.name)
            if self.wmax.get(key, -1) >= v:
                continue
            self.wmax[key] = v
            ws.append((s.h, v))
        it = None
        if inc is not None:
            inc.n += inc.step
            it = (inc.h, inc.step)
        def thunk(e, ws=tuple(ws), it=it, build=build):
            for h, v in ws:
                e.wait_ge(h, v)
            ins = build(e)
            if it is not None:
                if it[1] == 1:
                    ins.then_inc(it[0])
                else:
                    ins.then_inc(it[0], it[1])
        self.q[eng].append(thunk)
        return inc.tok() if inc is not None else None
    def waits_only(self, eng, waits):
        self.emit(eng, lambda e: None, waits=waits)
    def snapshot(self):
        return [(s, s.n) for s in self.sems if s.n > 0]

def build_nc(ph=None, taps=()):
    if ph is None:
        ph = DEBUG_PHASES
    nc = bass.Bass("TRN2", target_bir_lowering=False)
    C = SimpleNamespace()
    C.nc = nc
    C.ph = ph
    C.taps = tuple(taps)

    di = {}
    def din(name, shape, dt):
        di[name] = nc.dram_tensor(name, shape, dt, kind="ExternalInput")
        return di[name]
    x_in = din("x_in", [TX, D], F32)
    win_s = din("win_s", [PROJD, D], F32)
    wout_s = din("wout_s", [D, DI], F32)
    wbase_s = din("wbase_s", [D, D], F32)
    q_half = din("q_half", [D, T], F32)
    enc01_s = din("enc01_s", [128, NDC * R01], F32)
    c0_s_i = din("c0_s", [128, NDC * 96], F32)
    m0_s_i = din("m0_s", [128, NDC * 96], U8)
    c1_s_i = din("c1_s", [128, NDC * 24], F32)
    m1_s_i = din("m1_s", [128, NDC * 24], U8)
    convw_r = din("convw_r", [128, 320], F32)
    convb_r = din("convb_r", [128, 80], F32)
    dtb_c = din("dtb_c", [128, 1], F32)
    alog_c = din("alog_c", [128, 1], F32)
    dp_r = din("dp_r", [1, 128], F32)
    normw_r = din("normw_r", [128, NDC], F32)
    gw_r = din("gw_r", [1, DI], F32)
    bbase_r = din("bbase_r", [128, NDC], F32)
    ident_b = din("ident_b", [128, 128], BF16)
    ident_f = din("ident_f", [128, 128], F32)
    ones_f = din("ones_f", [1, 128], F32)
    ones_b = din("ones_b", [1, 128], BF16)
    trimask = din("trimask", [128, 128], BF16)
    parity = din("parity", [128, 1], F32)

    out_d = nc.dram_tensor("out", [T, D], F32, kind="ExternalOutput")

    ds = {}
    def dsc(name, shape, dt):
        ds[name] = nc.dram_tensor(name, shape, dt)
        return ds[name]
    xhat_d = dsc("xhat_d", [TX, D], BF16)
    xhi_d = dsc("xhi_d", [T, D], BF16)
    xlo_d = dsc("xlo_d", [T, D], BF16)
    hid_d = dsc("hid_d", [DI, T], BF16)
    b_d = dsc("b_d", [GRP * NS, T], BF16)
    c_d = dsc("c_d", [GRP * NS, T], BF16)
    sg_d = dsc("sg_d", [DI, T], BF16)
    acs_d = dsc("acs_d", [128, T], F32)
    s_d = dsc("s_d", [NCH * NS, DI], BF16)
    yg_d = dsc("yg_d", [T, DI], BF16)
    pre1h_d = dsc("pre1h_d", [D, T], BF16)
    pre1l_d = dsc("pre1l_d", [D, T], BF16)
    x1b_d = dsc("x1b_d", [T, D], BF16)
    x1f_d = dsc("x1f_d", [T, D], F32)
    ffnh_d = dsc("ffnh_d", [D, T], BF16)
    ffnl_d = dsc("ffnl_d", [D, T], BF16)
    cc_in = dsc("cc_in", [138, DI], F32)
    cc_out = dsc("cc_out", [276, DI], F32)

    tap_outs = {}
    for tname in C.taps:
        src = ds[tname]
        tap_outs[tname] = nc.dram_tensor(
            "dbg_" + tname, list(src.shape), src.dtype, kind="ExternalOutput")

    stack = ExitStack()
    with stack:
        pr = Prog(nc, stack)
        C.pr = pr

        te = pr.sem("te_s")
        ve = pr.sem("ve_s")
        se = pr.sem("se_s")
        po = pr.sem("po_s")
        sy_x = pr.sem("sy_x", 16)
        sy_m = pr.sem("sy_m", 16)
        sy_b0 = pr.sem("sy_b0", 16)
        sy_b1 = pr.sem("sy_b1", 16)
        sy_o = pr.sem("sy_o", 16)
        ge_w0 = pr.sem("ge_w0", 16)
        ge_w1 = pr.sem("ge_w1", 16)
        ge_m = pr.sem("ge_m", 16)
        ge_c = pr.sem("ge_c", 16)
        cc = pr.sem("cc_s", 1)
        se_t0 = pr.sem("se_t0", 16)
        se_t1 = pr.sem("se_t1", 16)
        sy_a0 = pr.sem("sy_a0", 16)
        sy_a1 = pr.sem("sy_a1", 16)
        sy_s0 = pr.sem("sy_s0", 16)
        sy_s1 = pr.sem("sy_s1", 16)
        se_tg = pr.sem("se_tg", 16)

        def barrier_all():
            snap = pr.snapshot()
            for eng in ENGS:
                pr.emit(eng, lambda e: None, waits=snap)

        E = stack.enter_context
        ident_bs = E(nc.sbuf_tensor("ident_bs", [128, 128], BF16))
        ident_fs = E(nc.sbuf_tensor("ident_fs", [128, 128], F32))
        ones_fs = E(nc.sbuf_tensor("ones_fs", [1, 128], F32))
        ones_bs = E(nc.sbuf_tensor("ones_bs", [1, 128], BF16))
        trimask_s = E(nc.sbuf_tensor("trimask_s", [128, 128], BF16))
        parity_s = E(nc.sbuf_tensor("parity_s", [128, 1], F32))
        convw_s = E(nc.sbuf_tensor("convw_s", [128, 320], F32))
        convb_s = E(nc.sbuf_tensor("convb_s", [128, 80], F32))
        dtb_s = E(nc.sbuf_tensor("dtb_s", [128, 1], F32))
        alog_s = E(nc.sbuf_tensor("alog_s", [128, 1], F32))
        nexpa_s = E(nc.sbuf_tensor("nexpa_s", [128, 1], F32))
        dp_s = E(nc.sbuf_tensor("dp_s", [1, 128], F32))
        normw_s = E(nc.sbuf_tensor("normw_s", [128, NDC], F32))
        bbase_s = E(nc.sbuf_tensor("bbase_s", [128, NDC], F32))
        acs_tok = E(nc.sbuf_tensor("acs_tok", [128, T], F32))
        expacs_tok = E(nc.sbuf_tensor("expacs_tok", [128, T], BF16))
        dt_tok = E(nc.sbuf_tensor("dt_tok", [128, T], BF16))
        w_tok = E(nc.sbuf_tensor("w_tok", [128, T], BF16))
        lrows = E(nc.sbuf_tensor("lrows", [1, NCH * 128], BF16))
        eps_col = E(nc.sbuf_tensor("eps_col", [128, 1], F32))
        one_col = E(nc.sbuf_tensor("one_col", [128, 1], F32))

        def ld(dst, src_ap):
            pr.emit("sy", lambda e, d=dst, s=src_ap: e.dma_start(out=d, in_=s),
                    inc=sy_m)
        ld(ident_bs[:, :], ident_b.ap())
        ld(ident_fs[:, :], ident_f.ap())
        ld(ones_fs[:, :], ones_f.ap())
        ld(ones_bs[:, :], ones_b.ap())
        ld(trimask_s[:, :], trimask.ap())
        ld(parity_s[:, :], parity.ap())
        ld(convw_s[:, :], convw_r.ap())
        ld(convb_s[:, :], convb_r.ap())
        ld(dtb_s[:, :], dtb_c.ap())
        ld(alog_s[:, :], alog_c.ap())
        ld(dp_s[:, :], dp_r.ap())
        ld(normw_s[:, :], normw_r.ap())
        ld(bbase_s[:, :], bbase_r.ap())
        tok_small = sy_m.tok()
        pr.emit("ge", lambda e: e.memset(eps_col[:, :], EPS), inc=po)
        pr.emit("ge", lambda e: e.memset(one_col[:, :], 1.0), inc=po)
        tok_cols = po.tok()
        pr.emit("se", lambda e: e.activation(nexpa_s[:, :], alog_s[:, :], AF.Exp),
                waits=[tok_small], inc=se)
        tok_nexpa = se.tok()

        p0s = ExitStack()
        xrow = [p0s.enter_context(nc.sbuf_tensor(f"xrow{i}", [128, D], F32)) for i in range(2)]
        ss0 = p0s.enter_context(nc.sbuf_tensor("ss0", [128, 1], F32))
        rstd0 = p0s.enter_context(nc.sbuf_tensor("rstd0", [128, 1], F32))
        xhatr = p0s.enter_context(nc.sbuf_tensor("xhatr", [128, D], BF16))
        xhir = p0s.enter_context(nc.sbuf_tensor("xhir", [128, D], BF16))
        xlor = p0s.enter_context(nc.sbuf_tensor("xlor", [128, D], BF16))
        p0ps = ExitStack()
        sq0 = p0ps.enter_context(nc.psum_tensor("sq0", [128, D], F32))

        NROW = TX // 128
        sy_bs = (sy_b0, sy_b1)
        for r in range(NROW):
            m = r % 2
            bs = sy_bs[m]
            pr.emit("sy", lambda e, m=m, r=r: e.dma_start(
                out=xrow[m][:, :], in_=x_in.ap()[r * 128:(r + 1) * 128, :]),
                waits=[ve, se], inc=sy_x)
            tok_x = sy_x.tok()
            pr.emit("se", lambda e, m=m: e.activation(
                sq0[:, :], xrow[m][:, :], AF.Square, accum_out=ss0[:, :]),
                waits=[tok_x, se, ve], inc=se)
            pr.emit("se", lambda e: e.activation(
                rstd0[:, :], ss0[:, :], AF.Sqrt, scale=1.0 / D, bias=eps_col[:, 0:1]),
                waits=[se, ve, tok_cols], inc=se)
            tok_sq = se.tok()
            pr.emit("ve", lambda e: e.reciprocal(rstd0[:, :], rstd0[:, :]),
                    waits=[tok_sq, ve], inc=ve)
            pr.emit("ve", lambda e, m=m: e.tensor_scalar(
                xhatr[:, :], xrow[m][:, :], rstd0[:, 0:1], None, op0=ALU.mult),
                waits=[ve, sy_b0, sy_b1], inc=ve)
            tok_xhat = ve.tok()
            pr.emit("sy", lambda e, r=r: e.dma_start(
                out=xhat_d.ap()[r * 128:(r + 1) * 128, :], in_=xhatr[:, :]),
                waits=[tok_xhat], inc=bs)
            if r >= 1:
                pr.emit("ve", lambda e, m=m: e.tensor_copy(xhir[:, :], xrow[m][:, :]),
                        waits=[ve, sy_b0, sy_b1], inc=ve)
                pr.emit("ve", lambda e, m=m: e.scalar_tensor_tensor(
                    xlor[:, :], xrow[m][:, :], 0.0, xhir[:, :],
                    op0=ALU.add, op1=ALU.subtract),
                    waits=[ve], inc=ve)
                tok_pair = ve.tok()
                pr.emit("sy", lambda e, r=r: e.dma_start(
                    out=xhi_d.ap()[(r - 1) * 128:r * 128, :], in_=xhir[:, :]),
                    waits=[tok_pair], inc=bs)
                pr.emit("sy", lambda e, r=r: e.dma_start(
                    out=xlo_d.ap()[(r - 1) * 128:r * 128, :], in_=xlor[:, :]),
                    waits=[tok_pair], inc=bs)
        tok_xhat_all = (sy_b0.tok(), sy_b1.tok())

        g1s = ExitStack()
        c0s = g1s.enter_context(nc.sbuf_tensor("c0s", [128, NDC * 96], F32))
        m0s = g1s.enter_context(nc.sbuf_tensor("m0s", [128, NDC * 96], U8))
        c1s = g1s.enter_context(nc.sbuf_tensor("c1s", [128, NDC * 24], F32))
        m1s = g1s.enter_context(nc.sbuf_tensor("m1s", [128, NDC * 24], U8))
        mf = g1s.enter_context(nc.sbuf_tensor("mf", [128, NDC * 96], F32))
        w01s = ExitStack()
        w01 = w01s.enter_context(nc.sbuf_tensor("w01", [128, NDC * R01], BF16, side="right"))

        ld(c0s[:, :], c0_s_i.ap())
        ld(m0s[:, :], m0_s_i.ap())
        ld(c1s[:, :], c1_s_i.ap())
        ld(m1s[:, :], m1_s_i.ap())
        tok_cm = sy_m.tok()
        pr.emit("ve", lambda e: e.tensor_copy(mf[:, :], m0s[:, :]),
                waits=[tok_cm, ve], inc=ve)
        pr.emit("ve", lambda e: e.tensor_mul(c0s[:, :], c0s[:, :], mf[:, :]),
                waits=[ve], inc=ve)
        pr.emit("ve", lambda e: e.tensor_copy(mf[:, 0:NDC * 24], m1s[:, :]),
                waits=[ve], inc=ve)
        pr.emit("ve", lambda e: e.tensor_mul(c1s[:, :], c1s[:, :], mf[:, 0:NDC * 24]),
                waits=[ve], inc=ve)
        def _e_ap(tsb, blk, rr, eidx):
            return AP(tsb, eidx * rr, [[NDC * blk, 128], [blk, NDC], [1, rr]])
        def _w01_ap(off, rr):
            return AP(w01, off, [[NDC * R01, 128], [R01, NDC], [1, rr]])
        pr.emit("ve", lambda e: e.tensor_add(
            _e_ap(c0s, 96, 32, 0), _e_ap(c0s, 96, 32, 0), _e_ap(c0s, 96, 32, 1)),
            waits=[ve], inc=ve)
        pr.emit("ve", lambda e: e.tensor_add(
            _w01_ap(0, 32), _e_ap(c0s, 96, 32, 0), _e_ap(c0s, 96, 32, 2)),
            waits=[ve], inc=ve)
        pr.emit("ve", lambda e: e.tensor_add(
            _e_ap(c1s, 24, 8, 0), _e_ap(c1s, 24, 8, 0), _e_ap(c1s, 24, 8, 1)),
            waits=[ve], inc=ve)
        pr.emit("ve", lambda e: e.tensor_add(
            _w01_ap(32, 8), _e_ap(c1s, 24, 8, 0), _e_ap(c1s, 24, 8, 2)),
            waits=[ve], inc=ve)
        tok_w01 = ve.tok()
        g1s.close()
        p0s.close()
        p0ps.close()

        hfs = ExitStack()
        hfeat = hfs.enter_context(nc.sbuf_tensor("hfeat", [128, NDC * TX], BF16))
        p0b = ExitStack()
        xts = [p0b.enter_context(nc.sbuf_tensor(f"xts{i}", [128, TX], BF16)) for i in range(2)]
        for c in range(NDC):
            m = c % 2
            st = (se_t0, se_t1)[m]
            pr.emit("se", lambda e, c=c, m=m: e.dma_start_transpose(
                xts[m][:, :], xhat_d.ap()[:, c * 128:(c + 1) * 128]),
                waits=[tok_xhat_all[0], tok_xhat_all[1], ve, tok_w01], inc=st)
            tok_t = st.tok()
            pr.emit("ve", lambda e, c=c, m=m: e.tensor_scalar(
                hfeat[:, c * TX:(c + 1) * TX], xts[m][:, :], normw_s[:, c:c + 1],
                None, op0=ALU.mult),
                waits=[tok_t, tok_small, ve], inc=ve)
        tok_hfeat = ve.tok()

        p1s = ExitStack()
        qstr = [p1s.enter_context(nc.sbuf_tensor(f"qstr{i}", [128, T], BF16)) for i in range(2)]
        ghalf = p1s.enter_context(nc.sbuf_tensor("ghalf", [R01, T], F32))
        p1ps = ExitStack()
        pg = p1ps.enter_context(nc.psum_tensor("pg", [R01, T], F32))
        ge_ws = (ge_w0, ge_w1)
        for c in range(NDC):
            m = c % 2
            gw = ge_ws[m]
            pr.emit("ge", lambda e, c=c, m=m: e.dma_start(
                out=qstr[m][:, :], in_=q_half.ap()[c * 128:(c + 1) * 128, :]),
                waits=[(te, te.n if c >= 2 else 0)], inc=gw)
            tok_q = gw.tok()
            for i, (o, w) in enumerate(NT_T):
                pr.emit("te", lambda e, c=c, m=m, o=o, w=w: e.matmul(
                    pg[:, o:o + w], w01[:, c * R01:(c + 1) * R01], qstr[m][:, o:o + w],
                    start=(c == 0), stop=(c == NDC - 1)),
                    waits=[tok_q, tok_w01],
                    inc=te if i == len(NT_T) - 1 else None)
        tok_g01mm = te.tok()
        pr.emit("ve", lambda e: e.tensor_scalar(
            ghalf[:, :], pg[:, :], 0.5, None, op0=ALU.mult),
            waits=[tok_g01mm, ve], inc=ve)
        tok_ghalf = ve.tok()
        pr.emit("ge", lambda e: e.dma_start(
            out=AP(cc_in, 128 * DI, [[T, R01], [1, T]]), in_=ghalf[:, :]),
            waits=[tok_ghalf], inc=ge_c)
        tok_ghalf_d = ge_c.tok()
        p1ps.close()
        p1s.close()
        w01s.close()
        p0b.close()

        if C.ph >= 2:
            p2s = ExitStack()
            wstr = [p2s.enter_context(nc.sbuf_tensor(f"wstr{i}", [128, NDC * 128], BF16)) for i in range(2)]
            dts = ExitStack()
            dt_ch = dts.enter_context(nc.sbuf_tensor("dt_ch", [128, T], F32))
            acs = dts.enter_context(nc.sbuf_tensor("acs", [128, T], F32))
            sp_a = dts.enter_context(nc.sbuf_tensor("sp_a", [128, T], F32))
            afull = sp_a
            zed = dts.enter_context(nc.sbuf_tensor("zed", [128, 128], F32))
            wch = dts.enter_context(nc.sbuf_tensor("wch", [128, 128], F32))
            tmpb = dts.enter_context(nc.sbuf_tensor("tmpb", [128, 128], BF16))
            tmpb2 = dts.enter_context(nc.sbuf_tensor("tmpb2", [128, 128], BF16))
            convs = [None]

            p2ps = ExitStack()
            pmain = p2ps.enter_context(nc.psum_tensor("pmain", [128, TX], F32))
            ptr_b = p2ps.enter_context(nc.psum_tensor("ptr_b", [128, 128], BF16))
            ptr_f = p2ps.enter_context(nc.psum_tensor("ptr_f", [128, 128], F32))

            pr.emit("ge", lambda e: e.memset(zed[:, :], 0.0), inc=po)

            blocks = [("dt", 0)] + [("bc", i) for i in range(16)] + [("hid", i) for i in range(64)]
            def blk_ob(kind, i):
                if kind == "dt":
                    return (DI + CONVD) // 128
                if kind == "bc":
                    return (DI + DI) // 128 + i
                return DI // 128 + i

            evac_tok = {}
            tap_tok = {}
            silu_tok = {}
            mm_tok = {}

            for k, (kind, bi) in enumerate(blocks):
                m = k % 2
                gw = ge_ws[m]
                ob = blk_ob(kind, bi)
                wts = ([mm_tok[k - 2]] if k >= 2
                       else [tok_hfeat, tok_g01mm, tok_ghalf, tok_ghalf_d, ve, se])
                pr.emit("ge", lambda e, m=m, ob=ob: e.dma_start(
                    out=wstr[m][:, :], in_=win_s.ap()[ob * 128:(ob + 1) * 128, :]),
                    waits=wts, inc=gw)
                tok_w = gw.tok()
                wl = [tok_w, tok_hfeat]
                if k == 0:
                    wl.append(tok_ghalf)
                if k >= 1:
                    wl.append(evac_tok[k - 1])
                for c in range(NDC):
                    for i, (o, w) in enumerate(NT_TX):
                        last = (c == NDC - 1 and i == len(NT_TX) - 1)
                        pr.emit("te", lambda e, c=c, m=m, o=o, w=w: e.matmul(
                            pmain[:, o:o + w], wstr[m][:, c * 128:(c + 1) * 128],
                            hfeat[:, c * TX + o:c * TX + o + w],
                            start=(c == 0), stop=(c == NDC - 1)),
                            waits=wl if (c == 0 and i == 0) else (),
                            inc=te if last else None)
                mm_tok[k] = te.tok()

                if kind == "dt":
                    pr.emit("se", lambda e: e.activation(
                        sp_a[:, :], pmain[:, LEAD:TX], AF.Abs, bias=dtb_s[:, 0:1]),
                        waits=[mm_tok[k], se], inc=se)
                    pr.emit("se", lambda e: e.activation(
                        dt_ch[:, :], pmain[:, LEAD:TX], AF.Relu, bias=dtb_s[:, 0:1]),
                        waits=[se], inc=se)
                    evac_tok[k] = se.tok()
                    pr.emit("se", lambda e: e.activation(
                        sp_a[:, :], sp_a[:, :], AF.Exp, scale=-1.0),
                        waits=[se], inc=se)
                    pr.emit("se", lambda e: e.activation(
                        sp_a[:, :], sp_a[:, :], AF.Ln, bias=one_col[:, 0:1]),
                        waits=[se, tok_cols], inc=se)
                    tok_sp = se.tok()
                    pr.emit("ve", lambda e: e.tensor_add(dt_ch[:, :], dt_ch[:, :], sp_a[:, :]),
                            waits=[tok_sp, ve], inc=ve)
                    tok_dt = ve.tok()
                    pr.emit("ve", lambda e: e.tensor_scalar(
                        afull[:, :], dt_ch[:, :], nexpa_s[:, 0:1], -1.0,
                        op0=ALU.mult, op1=ALU.mult),
                        waits=[tok_nexpa, ve], inc=ve)
                    for cch in range(NCH):
                        pr.emit("ve", lambda e, cch=cch: e.tensor_tensor_scan(
                            acs[:, cch * CHK:(cch + 1) * CHK],
                            afull[:, cch * CHK:(cch + 1) * CHK],
                            zed[:, :], 0.0, op0=ALU.add, op1=ALU.add),
                            waits=[ve, po], inc=ve)
                    tok_acs = ve.tok()
                    pr.emit("sy", lambda e: e.dma_start(out=acs_d.ap(), in_=acs[:, :]),
                            waits=[tok_acs], inc=sy_b0)
                    tok_acs_d = sy_b0.tok()
                    for cch in range(NCH):
                        cs = slice(cch * CHK, (cch + 1) * CHK)
                        pr.emit("se", lambda e, cs=cs, cch=cch: e.activation(
                            wch[:, :], acs[:, cs], AF.Exp, scale=-1.0,
                            bias=acs[:, cch * CHK + CHK - 1:cch * CHK + CHK]),
                            waits=[tok_acs, se], inc=se)
                        tok_wch = se.tok()
                        pr.emit("ve", lambda e, cs=cs: e.tensor_mul(
                            tmpb[:, :], wch[:, :], dt_ch[:, cs]),
                            waits=[tok_wch, ve, te], inc=ve)
                        tok_wb = ve.tok()
                        pr.emit("te", lambda e: e.transpose(
                            ptr_b[:, :], tmpb[:, :], ident_bs[:, :]),
                            waits=[tok_wb, tok_small, ve], inc=te)
                        tok_twb = te.tok()
                        pr.emit("ve", lambda e, cs=cs: e.tensor_copy(
                            w_tok[:, cs], ptr_b[:, :]), waits=[tok_twb, ve], inc=ve)
                        pr.emit("ve", lambda e, cs=cs: e.tensor_copy(
                            tmpb2[:, :], dt_ch[:, cs]), waits=[ve], inc=ve)
                        tok_db = ve.tok()
                        pr.emit("te", lambda e: e.transpose(
                            ptr_b[:, :], tmpb2[:, :], ident_bs[:, :]),
                            waits=[tok_db, ve], inc=te)
                        tok_tdb = te.tok()
                        pr.emit("ve", lambda e, cs=cs: e.tensor_copy(
                            dt_tok[:, cs], ptr_b[:, :]), waits=[tok_tdb, ve], inc=ve)
                        pr.emit("te", lambda e, cs=cs: e.transpose(
                            ptr_f[:, :], acs[:, cs], ident_fs[:, :]),
                            waits=[tok_acs, tok_small, ve], inc=te)
                        tok_taf = te.tok()
                        pr.emit("ve", lambda e, cs=cs: e.tensor_copy(
                            acs_tok[:, cs], ptr_f[:, :]), waits=[tok_taf, ve], inc=ve)
                        pr.emit("te", lambda e, cs=cs: e.matmul(
                            ptr_f[0:1, 0:128], ident_fs[:, 127:128],
                            acs_tok[:, cs], start=True, stop=True),
                            waits=[ve, se], inc=te)
                        tok_tlc = te.tok()
                        pr.emit("se", lambda e, cch=cch: e.activation(
                            lrows[0:1, cch * 128:(cch + 1) * 128],
                            AP(ptr_f, 0, [[128, 1], [1, 128]]), AF.Exp),
                            waits=[tok_tlc, se], inc=se)
                    pr.emit("se", lambda e: e.activation(
                        expacs_tok[:, :], acs_tok[:, :], AF.Exp),
                        waits=[ve, se], inc=se)
                    tap_tok[k] = ve.tok()
                    silu_tok[k] = se.tok()
                    dts.close()
                    tok_dtfree = pr.snapshot()
                    cvs = ExitStack()
                    p2s.enter_context(cvs)
                    pre = [cvs.enter_context(nc.sbuf_tensor("pre0", [128, TX], F32))] * 2
                    acc = [cvs.enter_context(nc.sbuf_tensor("acc0", [128, T], F32))] * 2
                    stg = [cvs.enter_context(nc.sbuf_tensor(f"stg{i}", [128, T], BF16)) for i in range(2)]
                    convs[0] = (pre, acc, stg)
                    continue

                cblk = bi if kind == "hid" else 64 + bi
                pre, acc, stg = convs[0]
                wl_ev = [mm_tok[k], se] + ([tap_tok[k - 1]] if k - 1 in tap_tok else [])
                if k == 1:
                    wl_ev = wl_ev + tok_dtfree
                pr.emit("se", lambda e, m=m: e.activation(
                    pre[m][:, :], pmain[:, :], AF.Copy),
                    waits=wl_ev, inc=se)
                evac_tok[k] = se.tok()
                eng = "ve"
                esem = ve
                wl0 = [evac_tok[k], tok_small, esem]
                if k == 1:
                    wl0 = wl0 + tok_dtfree
                if k - 1 in silu_tok:
                    wl0.append(silu_tok[k - 1])
                pr.emit(eng, lambda e, m=m, cblk=cblk: e.tensor_scalar(
                    acc[m][:, :], pre[m][:, 125:125 + T], convw_s[:, cblk * 4:cblk * 4 + 1],
                    None, op0=ALU.mult), waits=wl0, inc=esem)
                for kk in range(1, 4):
                    pr.emit(eng, lambda e, m=m, cblk=cblk, kk=kk: e.scalar_tensor_tensor(
                        acc[m][:, :], pre[m][:, 125 + kk:125 + kk + T],
                        convw_s[:, cblk * 4 + kk:cblk * 4 + kk + 1], acc[m][:, :],
                        op0=ALU.mult, op1=ALU.add), waits=[esem], inc=esem)
                tap_tok[k] = esem.tok()
                wl1 = [tap_tok[k], se]
                if k == 1:
                    wl1 = wl1 + tok_dtfree
                if k - 2 in silu_tok:
                    wl1.append((sy_bs[m], sy_bs[m].n))
                pr.emit("se", lambda e, m=m, cblk=cblk: e.activation(
                    stg[m][:, :], acc[m][:, :], AF.Silu, bias=convb_s[:, cblk:cblk + 1]),
                    waits=wl1, inc=se)
                silu_tok[k] = se.tok()
                if kind == "hid":
                    dst = hid_d.ap()[bi * 128:(bi + 1) * 128, :]
                elif bi < 8:
                    dst = b_d.ap()[bi * 128:(bi + 1) * 128, :]
                else:
                    dst = c_d.ap()[(bi - 8) * 128:(bi - 7) * 128, :]
                pr.emit("sy", lambda e, m=m, dst=dst: e.dma_start(out=dst, in_=stg[m][:, :]),
                        waits=[silu_tok[k]], inc=sy_bs[m])
            tok_p2_done = (sy_b0.tok(), sy_b1.tok(), te.tok(), ve.tok(), se.tok(), po.tok())
            p2ps.close()
            p2s.close()

        if C.ph >= 3:
            p24 = ExitStack()
            htk = [p24.enter_context(nc.sbuf_tensor(f"htk{i}", [128, 1024], BF16)) for i in range(2)]
            btk = [p24.enter_context(nc.sbuf_tensor(f"btk{i}", [128, GRP * 128], BF16)) for i in range(2)]
            sst = [p24.enter_context(nc.sbuf_tensor("sst0", [128, 1024], BF16))] * 2
            Fst = p24.enter_context(nc.sbuf_tensor("Fst", [128, DI], F32))
            lamb = p24.enter_context(nc.sbuf_tensor("lamb", [128, 128], F32))
            p24p = ExitStack()
            ps_s = p24p.enter_context(nc.psum_tensor("ps_s", [128, 1024], F32))
            pbc = p24p.enter_context(nc.psum_tensor("pbc", [128, 128], F32))

            barrier_all()
            pr.emit("ge", lambda e: e.memset(Fst[:, :], 0.0), inc=po)
            tok_f0 = po.tok()
            htok_cons = {}
            btok_cons = {}
            sst_cons = {}
            for cch in range(NCH):
                mb = cch % 2
                stb = (se_t0, se_t1)[mb]
                pr.emit("se", lambda e, cch=cch, mb=mb: e.dma_start_transpose(
                    btk[mb][:, :], b_d.ap()[:, cch * 128:(cch + 1) * 128]),
                    waits=[btok_cons.get(mb, (te, 0))], inc=stb)
                tok_bt = stb.tok()
                pr.emit("te", lambda e, cch=cch: e.matmul(
                    pbc[:, :], ones_bs[0:1, :],
                    lrows[0:1, cch * 128:(cch + 1) * 128], start=True, stop=True),
                    waits=[ve], inc=te)
                tok_lb = te.tok()
                pr.emit("ve", lambda e: e.tensor_copy(lamb[:, :], pbc[:, :]),
                        waits=[tok_lb, ve, po], inc=ve)
                tok_lamb = ve.tok()
                for g in range(GRP):
                    mh = (cch * GRP + g) % 2
                    sth = (sy_a0, sy_a1)[mh]
                    pr.emit("sy", lambda e, cch=cch, g=g, mh=mh: e.dma_start(
                        out=htk[mh][:, :],
                        in_=hid_d.ap()[g * 1024:(g + 1) * 1024, cch * 128:(cch + 1) * 128],
                        transpose=True),
                        waits=[htok_cons.get(("h", mh), (te, 0))], inc=sth)
                    tok_h = sth.tok()
                    pr.emit("ve", lambda e, cch=cch, g=g, mh=mh: e.tensor_tensor(
                        htk[mh][:, :], htk[mh][:, :],
                        AP(w_tok, cch * 128 + g * 16, [[T, 128], [1, 16], [0, 64]]),
                        op=ALU.mult),
                        waits=[tok_h, ve], inc=ve)
                    tok_hd = ve.tok()
                    for i, (o, w) in enumerate(ntiles(1024)):
                        pr.emit("te", lambda e, g=g, mb=mb, mh=mh, o=o, w=w: e.matmul(
                            ps_s[:, o:o + w], btk[mb][:, g * 128:(g + 1) * 128],
                            htk[mh][:, o:o + w], start=True, stop=True),
                            waits=[tok_hd, tok_bt, ve], inc=te if i == 1 else None)
                    tok_smm = te.tok()
                    htok_cons[("h", mh)] = (te, te.n)
                    btok_cons[mb] = (te, te.n)
                    pr.emit("ge", lambda e, g=g: e.tensor_tensor(
                        Fst[:, g * 1024:(g + 1) * 1024], Fst[:, g * 1024:(g + 1) * 1024],
                        AP(lamb, g * 16, [[128, 128], [1, 16], [0, 64]]),
                        op=ALU.mult),
                        waits=[tok_lamb, tok_f0, po], inc=po)
                    tok_fl = po.tok()
                    pr.emit("ve", lambda e, g=g: e.tensor_add(
                        Fst[:, g * 1024:(g + 1) * 1024], Fst[:, g * 1024:(g + 1) * 1024],
                        ps_s[:, :]),
                        waits=[tok_fl, tok_smm, ve], inc=ve)
                    tok_fa = ve.tok()
                    pr.emit("ve", lambda e: e.tensor_copy(sst[0][:, :], ps_s[:, :]),
                            waits=[tok_fa, ve, sy_s0], inc=ve)
                    tok_sev = ve.tok()
                    pr.emit("sy", lambda e, cch=cch, g=g: e.dma_start(
                        out=s_d.ap()[cch * 128:(cch + 1) * 128, g * 1024:(g + 1) * 1024],
                        in_=sst[0][:, :]),
                        waits=[tok_sev], inc=sy_s0)
            tok_f_done = ve.tok()
            pr.emit("ge", lambda e: e.dma_start(out=cc_in.ap()[0:128, :], in_=Fst[:, :]),
                    waits=[tok_f_done], inc=ge_c)
            tok_ccin = ge_c.tok()
            if DEBUG_NO_CC:
                pr.emit("ge", lambda e: e.dma_start(
                    out=cc_out.ap()[0:138, :], in_=cc_in.ap()),
                    waits=[tok_ccin], inc=ge_c)
                pr.emit("ge", lambda e: e.dma_start(
                    out=cc_out.ap()[138:276, :], in_=cc_in.ap()),
                    waits=[], inc=ge_c)
                tok_cc_done = ge_c.tok()
            else:
                pr.emit("ge", lambda e: e.collective_compute(
                    "AllGather", ALU.bypass,
                    replica_groups=[[0, 1], [2, 3], [4, 5], [6, 7]],
                    ins=[cc_in.ap().opt()], outs=[cc_out.ap().opt()]),
                    waits=[tok_ccin], inc=cc)
                tok_cc_done = cc.tok()
            p24p.close()
            p24.close()


        if C.ph >= 4:
            snap3 = [t for t in pr.snapshot() if t[0] is not cc]
            for eng in ENGS:
                pr.emit(eng, lambda e: None, waits=snap3)
            p3s = ExitStack()
            wstr3 = [p3s.enter_context(nc.sbuf_tensor(f"wstr3{i}", [128, NDC * 128], BF16)) for i in range(2)]
            gst = [p3s.enter_context(nc.sbuf_tensor(f"gst{i}", [128, T], BF16)) for i in range(2)]
            p3ps = ExitStack()
            pm3 = p3ps.enter_context(nc.psum_tensor("pm3", [128, T], F32))
            mm3_tok = {}
            ev3_tok = {}
            for kb in range(64):
                m = kb % 2
                gw = ge_ws[m]
                pr.emit("ge", lambda e, kb=kb, m=m: e.dma_start(
                    out=wstr3[m][:, :], in_=win_s.ap()[kb * 128:(kb + 1) * 128, :]),
                    waits=[mm3_tok[kb - 2]] if kb >= 2 else [], inc=gw)
                tok_w = gw.tok()
                wl = [tok_w]
                if kb >= 1:
                    wl.append(ev3_tok[kb - 1])
                for c in range(NDC):
                    for i, (o, w) in enumerate(NT_T):
                        last = (c == NDC - 1 and i == len(NT_T) - 1)
                        pr.emit("te", lambda e, c=c, m=m, o=o, w=w: e.matmul(
                            pm3[:, o:o + w], wstr3[m][:, c * 128:(c + 1) * 128],
                            hfeat[:, c * TX + LEAD + o:c * TX + LEAD + o + w],
                            start=(c == 0), stop=(c == NDC - 1)),
                            waits=wl if (c == 0 and i == 0) else (),
                            inc=te if last else None)
                mm3_tok[kb] = te.tok()
                bs = sy_bs[m]
                pr.emit("se", lambda e, m=m: e.activation(
                    gst[m][:, :], pm3[:, :], AF.Silu),
                    waits=[mm3_tok[kb], se, (bs, bs.n)], inc=se)
                ev3_tok[kb] = se.tok()
                pr.emit("sy", lambda e, m=m, kb=kb: e.dma_start(
                    out=sg_d.ap()[kb * 128:(kb + 1) * 128, :], in_=gst[m][:, :]),
                    waits=[ev3_tok[kb]], inc=bs)
            p3ps.close()
            p3s.close()
            hfs.close()

        if C.ph >= 5:
            barrier_all()
            p4s = ExitStack()
            gwb = p4s.enter_context(nc.sbuf_tensor("gwb", [128, DI], BF16))
            dpb = p4s.enter_context(nc.sbuf_tensor("dpb", [128, 128], F32))
            Pst = p4s.enter_context(nc.sbuf_tensor("Pst", [128, DI], F32))
            htk4 = [p4s.enter_context(nc.sbuf_tensor(f"htk4{i}", [128, DI], BF16)) for i in range(2)]
            hdt = p4s.enter_context(nc.sbuf_tensor("hdt", [128, DI], BF16))
            sgt = p4s.enter_context(nc.sbuf_tensor("sgt", [128, DI], BF16))
            sld = p4s.enter_context(nc.sbuf_tensor("sld", [128, DI], BF16))
            Ych = p4s.enter_context(nc.sbuf_tensor("Ych", [128, DI], BF16))
            cfeat = p4s.enter_context(nc.sbuf_tensor("cfeat", [128, GRP * 128], BF16))
            bfeat = p4s.enter_context(nc.sbuf_tensor("bfeat", [128, GRP * 128], BF16))
            mbc = p4s.enter_context(nc.sbuf_tensor("mbc", [128, 128], BF16))
            yt4 = p4s.enter_context(nc.sbuf_tensor("yt4", [128, 2048], BF16))
            Lt4 = p4s.enter_context(nc.sbuf_tensor("Lt4", [128, 2048], BF16))
            Mt4 = p4s.enter_context(nc.sbuf_tensor("Mt4", [128, 2048], BF16))
            arow = [p4s.enter_context(nc.sbuf_tensor("arow0", [1, 2048], F32))] * 2
            tmp1k = p4s.enter_context(nc.sbuf_tensor("tmp1k", [128, 1024], BF16))
            gsum = p4s.enter_context(nc.sbuf_tensor("gsum", [128, 8], F32))
            grstd = p4s.enter_context(nc.sbuf_tensor("grstd", [128, 8], F32))
            lamb4 = p4s.enter_context(nc.sbuf_tensor("lamb4", [128, 128], F32))
            gwr4 = p4s.enter_context(nc.sbuf_tensor("gwr4", [1, 1024], BF16))
            p4ps = ExitStack()
            pR = p4ps.enter_context(nc.psum_tensor("pR", [128, 2048], F32))
            poff = p4ps.enter_context(nc.psum_tensor("poff", [128, 1024], F32))
            pdiag = p4ps.enter_context(nc.psum_tensor("pdiag", [128, 1024], F32))

            for i in range(8):
                pr.emit("ge", lambda e, i=i: e.dma_start(
                    out=gwr4[:, :], in_=gw_r.ap()[:, i * 1024:(i + 1) * 1024]),
                    waits=[(te, te.n)], inc=ge_m)
                tok_gwr = ge_m.tok()
                for j, (o, w) in enumerate(ntiles(1024)):
                    pr.emit("te", lambda e, o=o, w=w: e.matmul(
                        pR[:, o:o + w], ones_bs[0:1, :], gwr4[0:1, o:o + w],
                        start=True, stop=True),
                        waits=[tok_gwr, ve], inc=te if j == 1 else None)
                tok_gwbmm = te.tok()
                pr.emit("ve", lambda e, i=i: e.tensor_scalar(
                    gwb[:, i * 1024:(i + 1) * 1024], pR[:, 0:1024], 0.5, None,
                    op0=ALU.mult),
                    waits=[tok_gwbmm, ve], inc=ve)
            tok_gwb = ve.tok()
            pr.emit("te", lambda e: e.matmul(
                pdiag[:, 0:128], ones_fs[0:1, :], dp_s[0:1, :],
                start=True, stop=True), waits=[tok_gwb], inc=te)
            tok_dpmm = te.tok()
            pr.emit("ve", lambda e: e.tensor_copy(dpb[:, :], pdiag[:, 0:128]),
                    waits=[tok_dpmm, ve], inc=ve)
            tok_dpb = ve.tok()

            se_ts = (se_t0, se_t1)
            sy_as = (sy_a0, sy_a1)
            htk_cons = {}
            sgt_cons = (ve, 0)
            sld_cons = (ve, 0)
            arow_cons = {}
            ych_bounce = {}
            for cch in range(NCH):
                mc = cch % 2
                pr.emit("se", lambda e, cch=cch, mc=mc: e.dma_start_transpose(
                    htk4[mc][:, :], hid_d.ap()[:, cch * 128:(cch + 1) * 128]),
                    waits=[htk_cons.get(mc, (te, 0))], inc=se_ts[mc])
                tok_h4 = se_ts[mc].tok()
                pr.emit("se", lambda e, cch=cch: e.dma_start_transpose(
                    sgt[:, :], sg_d.ap()[:, cch * 128:(cch + 1) * 128]),
                    waits=[sgt_cons], inc=se_tg)
                tok_sg4 = se_tg.tok()
                pr.emit("sy", lambda e, cch=cch: e.dma_start(
                    out=cfeat[:, :].rearrange("p (g i) -> p g i", g=GRP),
                    in_=AP(c_d, cch * 128, [[2048, 128], [128 * 2048, GRP], [1, 128]])),
                    waits=[(te, te.n)], inc=sy_m)
                pr.emit("sy", lambda e, cch=cch: e.dma_start(
                    out=bfeat[:, :].rearrange("p (g i) -> p g i", g=GRP),
                    in_=AP(b_d, cch * 128, [[2048, 128], [128 * 2048, GRP], [1, 128]])),
                    waits=[], inc=sy_m)
                tok_cb4 = sy_m.tok()
                if cch == 0:
                    pr.emit("sy", lambda e: e.dma_start(
                        out=Pst[:, :], in_=cc_out.ap()[0:128, :]),
                        waits=[tok_cc_done], inc=sy_s1)
                    tok_pload = sy_s1.tok()
                    pr.emit("ve", lambda e: e.tensor_scalar(
                        Pst[:, :], Pst[:, :], parity_s[:, 0:1], None, op0=ALU.mult),
                        waits=[tok_pload, tok_small, ve], inc=ve)
                else:
                    pr.emit("sy", lambda e, cch=cch: e.dma_start(
                        out=sld[:, :], in_=s_d.ap()[(cch - 1) * 128:cch * 128, :]),
                        waits=[sld_cons], inc=sy_s1)
                    tok_sld = sy_s1.tok()
                    pr.emit("te", lambda e, cch=cch: e.matmul(
                        pdiag[:, 0:128], ones_bs[0:1, :],
                        lrows[0:1, (cch - 1) * 128:cch * 128],
                        start=True, stop=True), waits=[ve], inc=te)
                    tok_lmm = te.tok()
                    pr.emit("ve", lambda e: e.tensor_copy(lamb4[:, :], pdiag[:, 0:128]),
                            waits=[tok_lmm, ve], inc=ve)
                    tok_l4 = ve.tok()
                    pr.emit("ge", lambda e: e.tensor_tensor(
                        Pst[:, :], Pst[:, :],
                        AP(lamb4, 0, [[128, 128], [1, 128], [0, PDIM]]),
                        op=ALU.mult),
                        waits=[tok_l4, ve, po], inc=po)
                    tok_pl = po.tok()
                    pr.emit("ve", lambda e: e.tensor_add(
                        Pst[:, :], Pst[:, :], sld[:, :]),
                        waits=[tok_pl, tok_sld, ve], inc=ve)
                tok_pst = ve.tok()
                pr.emit("ve", lambda e: e.tensor_copy(sld[:, :], Pst[:, :]),
                        waits=[tok_pst, ve], inc=ve)
                tok_pb = ve.tok()
                pr.emit("ve", lambda e, cch=cch, mc=mc: e.tensor_tensor(
                    hdt[:, :], htk4[mc][:, :],
                    AP(dt_tok, cch * 128, [[T, 128], [1, 128], [0, PDIM]]),
                    op=ALU.mult),
                    waits=[tok_h4, ve, se], inc=ve)
                tok_hdt = ve.tok()
                for g in range(GRP):
                    ma = (cch * GRP + g) % 2
                    pr.emit("sy", lambda e, cch=cch, g=g, ma=ma: e.dma_start(
                        out=arow[0][0:1, :].rearrange("p (h i) -> p h i", h=16),
                        in_=AP(acs_d, (16 * g) * 2048 + cch * 128,
                               [[262144, 1], [2048, 16], [1, 128]])),
                        waits=[arow_cons.get(0, (te, 0))], inc=sy_a0)
                    tok_ar = sy_a0.tok()
                    pr.emit("te", lambda e, g=g: e.matmul(
                        pdiag[:, 0:128], bfeat[:, g * 128:(g + 1) * 128],
                        cfeat[:, g * 128:(g + 1) * 128], start=True, stop=True),
                        waits=[tok_cb4, ve], inc=te)
                    tok_bc = te.tok()
                    pr.emit("ve", lambda e: e.tensor_mul(
                        mbc[:, :], pdiag[:, 0:128], trimask_s[:, :]),
                        waits=[tok_bc, ve], inc=ve)
                    tok_mbc = ve.tok()
                    for i, (o, w) in enumerate(ntiles(2048)):
                        pr.emit("te", lambda e, ma=ma, o=o, w=w: e.matmul(
                            pR[:, o:o + w], ones_fs[0:1, :], arow[ma][0:1, o:o + w],
                            start=True, stop=True),
                            waits=([tok_ar, ve, se] if g == 0 else [tok_ar, ve]),
                            inc=te if i == 3 else None)
                    tok_r = te.tok()
                    arow_cons[0] = (te, te.n)
                    pr.emit("ve", lambda e, cch=cch, g=g: e.scalar_tensor_tensor(
                        yt4[:, :], pR[:, :], 0.0,
                        AP(acs_tok, cch * 128 + 16 * g, [[T, 128], [1, 16], [0, 128]]),
                        op0=ALU.add, op1=ALU.subtract),
                        waits=[tok_r, ve], inc=ve)
                    pr.emit("ve", lambda e: e.tensor_scalar_min(
                        yt4[:, :], yt4[:, :], 0.0), waits=[ve], inc=ve)
                    tok_y = ve.tok()
                    pr.emit("se", lambda e: e.activation(
                        Lt4[:, :], yt4[:, :], AF.Exp),
                        waits=[tok_y, se], inc=se)
                    tok_L = se.tok()
                    pr.emit("ve", lambda e: e.tensor_tensor(
                        Mt4[:, :], Lt4[:, :],
                        AP(mbc, 0, [[128, 128], [0, 16], [1, 128]]),
                        op=ALU.mult),
                        waits=[tok_L, tok_mbc, ve], inc=ve)
                    tok_M = ve.tok()
                    for hh in range(16):
                        pr.emit("te", lambda e, g=g, hh=hh: e.matmul(
                            poff[:, hh * PDIM:(hh + 1) * PDIM],
                            cfeat[:, g * 128:(g + 1) * 128],
                            sld[:, (16 * g + hh) * PDIM:(16 * g + hh + 1) * PDIM],
                            start=True, stop=True),
                            waits=[tok_pb, tok_M] if hh == 0 else (),
                            inc=te if hh == 15 else None)
                    tok_off = te.tok()
                    for hh in range(16):
                        pr.emit("te", lambda e, g=g, hh=hh: e.matmul(
                            pdiag[:, hh * PDIM:(hh + 1) * PDIM],
                            Mt4[:, hh * 128:(hh + 1) * 128],
                            hdt[:, (16 * g + hh) * PDIM:(16 * g + hh + 1) * PDIM],
                            start=True, stop=True),
                            waits=[tok_hdt] if hh == 0 else (),
                            inc=te if hh == 15 else None)
                    tok_diag = te.tok()
                    pr.emit("ve", lambda e, cch=cch, g=g: e.scalar_tensor_tensor(
                        tmp1k[:, :], poff[:, :], 0.0,
                        AP(expacs_tok, cch * 128 + 16 * g, [[T, 128], [1, 16], [0, PDIM]]),
                        op0=ALU.add, op1=ALU.mult),
                        waits=[tok_off, ve, po], inc=ve)
                    pr.emit("ve", lambda e, g=g: e.tensor_add(
                        Ych[:, g * 1024:(g + 1) * 1024], pdiag[:, :], tmp1k[:, :]),
                        waits=[tok_diag, ve, ych_bounce.get(0, (ve, 0))], inc=ve)
                tok_yall = ve.tok()
                sld_cons = (te, te.n)
                tok_gate = None
                for g in range(GRP):
                    sl = slice(g * 1024, (g + 1) * 1024)
                    pr.emit("ge", lambda e, sl=sl, mc=mc: e.tensor_tensor(
                        hdt[:, sl], htk4[mc][:, sl],
                        AP(dpb, 16 * (sl.start // 1024), [[128, 128], [1, 16], [0, PDIM]]),
                        op=ALU.mult),
                        waits=[tok_yall, tok_dpb, po], inc=po)
                    pr.emit("ge", lambda e, sl=sl: e.tensor_add(
                        hdt[:, sl], hdt[:, sl], Ych[:, sl]),
                        waits=[po], inc=po)
                    pr.emit("ge", lambda e, sl=sl: e.tensor_mul(
                        hdt[:, sl], hdt[:, sl], sgt[:, sl]),
                        waits=[tok_sg4, po], inc=po)
                tok_yg = po.tok()
                sgt_cons = (po, po.n)
                htk_cons[mc] = (po, po.n)
                for gg in range(8):
                    pr.emit("se", lambda e, gg=gg: e.activation(
                        pR[:, 0:1024], hdt[:, gg * 1024:(gg + 1) * 1024], AF.Square,
                        accum_out=gsum[:, gg:gg + 1]),
                        waits=[tok_yg, se, ve, te], inc=se)
                tok_gs = se.tok()
                pr.emit("se", lambda e: e.activation(
                    grstd[:, :], gsum[:, :], AF.Sqrt, scale=1.0 / 1024.0,
                    bias=eps_col[:, 0:1]), waits=[tok_gs, se], inc=se)
                tok_gr = se.tok()
                pr.emit("ve", lambda e: e.reciprocal(grstd[:, :], grstd[:, :]),
                        waits=[tok_gr, ve], inc=ve)
                tok_gri = ve.tok()
                for gg in range(8):
                    pr.emit("ve", lambda e, gg=gg: e.scalar_tensor_tensor(
                        Ych[:, gg * 1024:(gg + 1) * 1024],
                        hdt[:, gg * 1024:(gg + 1) * 1024], grstd[:, gg:gg + 1],
                        gwb[:, gg * 1024:(gg + 1) * 1024],
                        op0=ALU.mult, op1=ALU.mult),
                        waits=[tok_gri, tok_gwb, ve], inc=ve)
                tok_ygn = ve.tok()
                bs = sy_bs[cch % 2]
                pr.emit("sy", lambda e, cch=cch: e.dma_start(
                    out=yg_d.ap()[cch * 128:(cch + 1) * 128, :], in_=Ych[:, :]),
                    waits=[tok_ygn], inc=bs)
                ych_bounce[0] = (bs, bs.n)
            p4ps.close()
            p4s.close()

        if C.ph >= 6:
            barrier_all()
            p5s = ExitStack()
            ygf = p5s.enter_context(nc.sbuf_tensor("ygf", [128, 64 * 1024], BF16))
            wstr5 = [p5s.enter_context(nc.sbuf_tensor(f"wstr5{i}", [128, DI], BF16)) for i in range(2)]
            xph = [p5s.enter_context(nc.sbuf_tensor(f"xph{i}", [128, 1024], BF16)) for i in range(2)]
            xpl = [p5s.enter_context(nc.sbuf_tensor(f"xpl{i}", [128, 1024], BF16)) for i in range(2)]
            ptmp = p5s.enter_context(nc.sbuf_tensor("ptmp", [128, 1024], F32))
            hist = [p5s.enter_context(nc.sbuf_tensor("hist0", [128, 1024], BF16))] * 2
            lost = [p5s.enter_context(nc.sbuf_tensor("lost0", [128, 1024], BF16))] * 2
            p5ps = ExitStack()
            po5 = [p5ps.enter_context(nc.psum_tensor(f"po5_{i}", [128, 1024], F32)) for i in range(4)]
            ev5 = {}
            mm5 = {}
            for grp in range(2):
                for ic in range(64):
                    pr.emit("se", lambda e, grp=grp, ic=ic: e.dma_start_transpose(
                        ygf[:, ic * 1024:(ic + 1) * 1024],
                        yg_d.ap()[grp * 1024:(grp + 1) * 1024, ic * 128:(ic + 1) * 128]),
                        waits=[(te, te.n)] if ic == 0 else (), inc=se_t0)
                tok_ygf = se_t0.tok()
                for ob in range(32):
                    m = ob % 2
                    gw = ge_ws[m]
                    pr.emit("ge", lambda e, ob=ob, m=m: e.dma_start(
                        out=wstr5[m][:, :], in_=wout_s.ap()[ob * 128:(ob + 1) * 128, :]),
                        waits=[mm5[(grp, ob - 2)]] if ob >= 2 else
                              ([mm5[(grp - 1, 30 + ob)]] if grp else []),
                        inc=gw)
                    tok_w5 = gw.tok()
                    pb = po5[ob % 4]
                    wl = [tok_w5, tok_ygf]
                    if (grp, ob - 4) in ev5:
                        wl.append(ev5[(grp, ob - 4)])
                    elif grp and ob < 4:
                        wl.append(ev5[(grp - 1, 28 + ob)])
                    for ic in range(64):
                        for i, (o, w) in enumerate(ntiles(1024)):
                            last = (ic == 63 and i == 1)
                            pr.emit("te", lambda e, m=m, ic=ic, o=o, w=w, pb=pb: e.matmul(
                                pb[:, o:o + w], wstr5[m][:, ic * 128:(ic + 1) * 128],
                                ygf[:, ic * 1024 + o:ic * 1024 + o + w],
                                start=(ic == 0), stop=(ic == 63)),
                                waits=wl if (ic == 0 and i == 0) else (),
                                inc=te if last else None)
                    mm5[(grp, ob)] = te.tok()
                    pr.emit("se", lambda e, grp=grp, ob=ob, m=m: e.dma_start_transpose(
                        xph[m][:, :],
                        xhi_d.ap()[grp * 1024:(grp + 1) * 1024, ob * 128:(ob + 1) * 128]),
                        waits=[(ve, ve.n)], inc=se_t1)
                    pr.emit("se", lambda e, grp=grp, ob=ob, m=m: e.dma_start_transpose(
                        xpl[m][:, :],
                        xlo_d.ap()[grp * 1024:(grp + 1) * 1024, ob * 128:(ob + 1) * 128]),
                        waits=[], inc=se_t1)
                    tok_xp = se_t1.tok()
                    pr.emit("ve", lambda e, pb=pb, m=m: e.tensor_add(
                        ptmp[:, :], pb[:, :], xph[m][:, :]),
                        waits=[mm5[(grp, ob)], tok_xp, ve], inc=ve)
                    pr.emit("ve", lambda e, m=m: e.tensor_add(
                        ptmp[:, :], ptmp[:, :], xpl[m][:, :]),
                        waits=[ve], inc=ve)
                    ev5[(grp, ob)] = ve.tok()
                    bs = sy_bs[m]
                    pr.emit("ve", lambda e, m=m: e.tensor_copy(hist[m][:, :], ptmp[:, :]),
                            waits=[ve, sy_b0, sy_b1], inc=ve)
                    pr.emit("ve", lambda e, m=m: e.scalar_tensor_tensor(
                        lost[m][:, :], ptmp[:, :], 0.0, hist[m][:, :],
                        op0=ALU.add, op1=ALU.subtract),
                        waits=[ve], inc=ve)
                    tok_hl = ve.tok()
                    pr.emit("sy", lambda e, grp=grp, ob=ob, m=m: e.dma_start(
                        out=pre1h_d.ap()[ob * 128:(ob + 1) * 128,
                                         grp * 1024:(grp + 1) * 1024],
                        in_=hist[m][:, :]), waits=[tok_hl], inc=bs)
                    pr.emit("sy", lambda e, grp=grp, ob=ob, m=m: e.dma_start(
                        out=pre1l_d.ap()[ob * 128:(ob + 1) * 128,
                                         grp * 1024:(grp + 1) * 1024],
                        in_=lost[m][:, :]), waits=[], inc=bs)
            p5ps.close()
            p5s.close()

        if C.ph >= 6:
            barrier_all()
            p55 = ExitStack()
            p1h = [p55.enter_context(nc.sbuf_tensor(f"p1h{i}", [128, D], BF16)) for i in range(2)]
            p1l = [p55.enter_context(nc.sbuf_tensor(f"p1l{i}", [128, D], BF16)) for i in range(2)]
            sfull = p55.enter_context(nc.sbuf_tensor("sfull", [128, D], F32))
            x1bt = p55.enter_context(nc.sbuf_tensor("x1bt", [128, D], BF16))
            x1ft = p55.enter_context(nc.sbuf_tensor("x1ft", [128, D], F32))
            ssq = p55.enter_context(nc.sbuf_tensor("ssq", [128, 1], F32))
            smean = p55.enter_context(nc.sbuf_tensor("smean", [128, 1], F32))
            svar = p55.enter_context(nc.sbuf_tensor("svar", [128, 1], F32))
            srstd = p55.enter_context(nc.sbuf_tensor("srstd", [128, 1], F32))
            su = p55.enter_context(nc.sbuf_tensor("su", [128, 1], F32))
            p55p = ExitStack()
            sq5 = p55p.enter_context(nc.psum_tensor("sq5", [128, D], F32))

            def emit_ln(src_f32, tok_src, out_pairs):
                pr.emit("se", lambda e: e.activation(
                    sq5[:, :], src_f32[:, :], AF.Square, accum_out=ssq[:, :]),
                    waits=[tok_src, se, ve], inc=se)
                tok_q = se.tok()
                pr.emit("ve", lambda e: e.reduce_sum(
                    smean[:, :], src_f32[:, :], axis=mybir.AxisListType.X),
                    waits=[tok_src, ve, se], inc=ve)
                pr.emit("ve", lambda e: e.tensor_scalar(
                    smean[:, :], smean[:, :], 1.0 / D, None, op0=ALU.mult),
                    waits=[ve], inc=ve)
                pr.emit("ve", lambda e: e.tensor_mul(svar[:, :], smean[:, :], smean[:, :]),
                        waits=[ve], inc=ve)
                pr.emit("ve", lambda e: e.scalar_tensor_tensor(
                    svar[:, :], ssq[:, :], 1.0 / D, svar[:, :],
                    op0=ALU.mult, op1=ALU.subtract),
                    waits=[tok_q, ve], inc=ve)
                tok_v = ve.tok()
                pr.emit("se", lambda e: e.activation(
                    srstd[:, :], svar[:, :], AF.Sqrt, bias=eps_col[:, 0:1]),
                    waits=[tok_v, se], inc=se)
                tok_sd = se.tok()
                pr.emit("ve", lambda e: e.reciprocal(srstd[:, :], srstd[:, :]),
                        waits=[tok_sd, ve], inc=ve)
                pr.emit("ve", lambda e: e.tensor_scalar(
                    su[:, :], smean[:, :], srstd[:, 0:1], -1.0,
                    op0=ALU.mult, op1=ALU.mult),
                    waits=[ve], inc=ve)
                toks = []
                for buf in out_pairs:
                    pr.emit("ve", lambda e, buf=buf: e.tensor_scalar(
                        buf[:, :], src_f32[:, :], srstd[:, 0:1], su[:, 0:1],
                        op0=ALU.mult, op1=ALU.add),
                        waits=[ve, sy_b0, sy_b1], inc=ve)
                    toks.append(ve.tok())
                return toks

            for tb in range(16):
                m = tb % 2
                pr.emit("se", lambda e, tb=tb, m=m: e.dma_start_transpose(
                    p1h[m][:, :], pre1h_d.ap()[:, tb * 128:(tb + 1) * 128]),
                    waits=[(ve, ve.n)], inc=se_t0)
                pr.emit("se", lambda e, tb=tb, m=m: e.dma_start_transpose(
                    p1l[m][:, :], pre1l_d.ap()[:, tb * 128:(tb + 1) * 128]),
                    waits=[], inc=se_t0)
                tok_p1 = se_t0.tok()
                pr.emit("ve", lambda e, m=m: e.tensor_add(
                    sfull[:, :], p1h[m][:, :], p1l[m][:, :]),
                    waits=[tok_p1, ve, se], inc=ve)
                tok_s = ve.tok()
                t_out = emit_ln(sfull, tok_s, [x1bt, x1ft])
                bs = sy_bs[m]
                pr.emit("sy", lambda e, tb=tb: e.dma_start(
                    out=x1b_d.ap()[tb * 128:(tb + 1) * 128, :], in_=x1bt[:, :]),
                    waits=[t_out[0]], inc=bs)
                pr.emit("sy", lambda e, tb=tb: e.dma_start(
                    out=x1f_d.ap()[tb * 128:(tb + 1) * 128, :], in_=x1ft[:, :]),
                    waits=[t_out[1]], inc=bs)
            p55p.close()
            p55.close()

        if C.ph >= 7:
            barrier_all()
            p6s = ExitStack()
            x1ff = p6s.enter_context(nc.sbuf_tensor("x1ff", [128, NDC * T], BF16))
            wstr6 = [p6s.enter_context(nc.sbuf_tensor(f"wstr6{i}", [128, D], BF16)) for i in range(2)]
            g01sb = p6s.enter_context(nc.sbuf_tensor("g01sb", [R01, D], BF16))
            e01 = p6s.enter_context(nc.sbuf_tensor("e01", [R01, T], BF16))
            enc01b = p6s.enter_context(nc.sbuf_tensor("enc01b", [128, NDC * R01], BF16))
            t6 = p6s.enter_context(nc.sbuf_tensor("t6", [128, T], F32))
            hi6 = [p6s.enter_context(nc.sbuf_tensor("hi60", [128, T], BF16))] * 2
            lo6 = [p6s.enter_context(nc.sbuf_tensor("lo60", [128, T], BF16))] * 2
            for dc in range(NDC):
                pr.emit("se", lambda e, dc=dc: e.dma_start_transpose(
                    x1ff[:, dc * T:(dc + 1) * T],
                    x1b_d.ap()[:, dc * 128:(dc + 1) * 128]),
                    waits=[], inc=se_t0)
            tok_x1f = se_t0.tok()
            pr.emit("ge", lambda e: e.dma_start(
                out=g01sb[:, 0:T], in_=AP(cc_out, 128 * DI, [[T, R01], [1, T]])),
                waits=[tok_cc_done], inc=ge_m)
            pr.emit("ge", lambda e: e.dma_start(
                out=g01sb[:, T:D], in_=AP(cc_out, (138 + 128) * DI, [[T, R01], [1, T]])),
                waits=[], inc=ge_m)
            pr.emit("ge", lambda e: e.dma_start(
                out=enc01b[:, :], in_=enc01_s.ap()), waits=[], inc=ge_m)
            tok_g01 = ge_m.tok()
            p6e = ExitStack()
            pe01 = p6e.enter_context(nc.psum_tensor("pe01", [R01, T], F32))
            for dc in range(NDC):
                for i, (o, w) in enumerate(NT_T):
                    pr.emit("te", lambda e, dc=dc, o=o, w=w: e.matmul(
                        pe01[:, o:o + w], enc01b[:, dc * R01:(dc + 1) * R01],
                        x1ff[:, dc * T + o:dc * T + o + w],
                        start=(dc == 0), stop=(dc == NDC - 1)),
                        waits=[tok_x1f, tok_g01] if (dc == 0 and i == 0) else (),
                        inc=te if (dc == NDC - 1 and i == 3) else None)
            tok_e01mm = te.tok()
            pr.emit("ve", lambda e: e.tensor_copy(e01[:, :], pe01[:, :]),
                    waits=[tok_e01mm, ve], inc=ve)
            tok_e01 = ve.tok()
            p6e.close()
            p6ps = ExitStack()
            p6p = [p6ps.enter_context(nc.psum_tensor(f"p6_{i}", [128, T], F32)) for i in range(2)]
            mm6 = {}
            ev6 = {}
            for ob in range(NDC):
                m = ob % 2
                gw = ge_ws[m]
                pr.emit("ge", lambda e, ob=ob, m=m: e.dma_start(
                    out=wstr6[m][:, :], in_=wbase_s.ap()[ob * 128:(ob + 1) * 128, :]),
                    waits=[mm6[ob - 2]] if ob >= 2 else [tok_e01mm], inc=gw)
                tok_w6 = gw.tok()
                pb = p6p[m]
                wl = [tok_w6, tok_e01]
                if ob - 2 in ev6:
                    wl.append(ev6[ob - 2])
                for dc in range(NDC):
                    for i, (o, w) in enumerate(NT_T):
                        pr.emit("te", lambda e, m=m, dc=dc, o=o, w=w, pb=pb: e.matmul(
                            pb[:, o:o + w], wstr6[m][:, dc * 128:(dc + 1) * 128],
                            x1ff[:, dc * T + o:dc * T + o + w],
                            start=(dc == 0), stop=False),
                            waits=wl if (dc == 0 and i == 0) else (),
                            inc=None)
                for i, (o, w) in enumerate(ntiles(T)):
                    pr.emit("te", lambda e, ob=ob, o=o, w=w, pb=pb: e.matmul(
                        pb[:, o:o + w], g01sb[0:R01, ob * 128:(ob + 1) * 128],
                        e01[0:R01, o:o + w], start=False, stop=True),
                        inc=te if i == 3 else None)
                mm6[ob] = te.tok()
                pr.emit("ve", lambda e, ob=ob, pb=pb: e.tensor_scalar(
                    t6[:, :], pb[:, :], bbase_s[:, ob:ob + 1], None, op0=ALU.add),
                    waits=[mm6[ob], tok_small, ve], inc=ve)
                bs = sy_bs[m]
                pr.emit("ve", lambda e, m=m: e.tensor_copy(hi6[m][:, :], t6[:, :]),
                        waits=[ve, sy_b0, sy_b1], inc=ve)
                pr.emit("ve", lambda e, m=m: e.scalar_tensor_tensor(
                    lo6[m][:, :], t6[:, :], 0.0, hi6[m][:, :],
                    op0=ALU.add, op1=ALU.subtract),
                    waits=[ve], inc=ve)
                ev6[ob] = ve.tok()
                pr.emit("sy", lambda e, ob=ob, m=m: e.dma_start(
                    out=ffnh_d.ap()[ob * 128:(ob + 1) * 128, :], in_=hi6[m][:, :]),
                    waits=[ev6[ob]], inc=bs)
                pr.emit("sy", lambda e, ob=ob, m=m: e.dma_start(
                    out=ffnl_d.ap()[ob * 128:(ob + 1) * 128, :], in_=lo6[m][:, :]),
                    waits=[], inc=bs)
            p6ps.close()
            p6s.close()

        if C.ph >= 7:
            barrier_all()
            p65 = ExitStack()
            f6h = [p65.enter_context(nc.sbuf_tensor(f"f6h{i}", [128, D], BF16)) for i in range(2)]
            f6l = [p65.enter_context(nc.sbuf_tensor(f"f6l{i}", [128, D], BF16)) for i in range(2)]
            x1r = [p65.enter_context(nc.sbuf_tensor(f"x1r{i}", [128, D], F32)) for i in range(2)]
            sful2 = p65.enter_context(nc.sbuf_tensor("sful2", [128, D], F32))
            outr = p65.enter_context(nc.sbuf_tensor("outr", [128, D], F32))
            ssq2 = p65.enter_context(nc.sbuf_tensor("ssq2", [128, 1], F32))
            smean2 = p65.enter_context(nc.sbuf_tensor("smean2", [128, 1], F32))
            svar2 = p65.enter_context(nc.sbuf_tensor("svar2", [128, 1], F32))
            srstd2 = p65.enter_context(nc.sbuf_tensor("srstd2", [128, 1], F32))
            su2 = p65.enter_context(nc.sbuf_tensor("su2", [128, 1], F32))
            p65p = ExitStack()
            sq6 = p65p.enter_context(nc.psum_tensor("sq6", [128, D], F32))
            for tb in range(16):
                m = tb % 2
                pr.emit("se", lambda e, tb=tb, m=m: e.dma_start_transpose(
                    f6h[m][:, :], ffnh_d.ap()[:, tb * 128:(tb + 1) * 128]),
                    waits=[(ve, ve.n)], inc=se_t0)
                pr.emit("se", lambda e, tb=tb, m=m: e.dma_start_transpose(
                    f6l[m][:, :], ffnl_d.ap()[:, tb * 128:(tb + 1) * 128]),
                    waits=[], inc=se_t0)
                tok_f6 = se_t0.tok()
                pr.emit("sy", lambda e, tb=tb, m=m: e.dma_start(
                    out=x1r[m][:, :], in_=x1f_d.ap()[tb * 128:(tb + 1) * 128, :]),
                    waits=[(ve, ve.n)], inc=sy_x)
                tok_x1r = sy_x.tok()
                pr.emit("ve", lambda e, m=m: e.tensor_add(
                    sful2[:, :], f6h[m][:, :], f6l[m][:, :]),
                    waits=[tok_f6, ve, se], inc=ve)
                pr.emit("ve", lambda e, m=m: e.tensor_add(
                    sful2[:, :], sful2[:, :], x1r[m][:, :]),
                    waits=[tok_x1r, ve], inc=ve)
                tok_s2 = ve.tok()
                pr.emit("se", lambda e: e.activation(
                    sq6[:, :], sful2[:, :], AF.Square, accum_out=ssq2[:, :]),
                    waits=[tok_s2, se, ve], inc=se)
                tok_q2 = se.tok()
                pr.emit("ve", lambda e: e.reduce_sum(
                    smean2[:, :], sful2[:, :], axis=mybir.AxisListType.X),
                    waits=[tok_s2, ve, se], inc=ve)
                pr.emit("ve", lambda e: e.tensor_scalar(
                    smean2[:, :], smean2[:, :], 1.0 / D, None, op0=ALU.mult),
                    waits=[ve], inc=ve)
                pr.emit("ve", lambda e: e.tensor_mul(
                    svar2[:, :], smean2[:, :], smean2[:, :]), waits=[ve], inc=ve)
                pr.emit("ve", lambda e: e.scalar_tensor_tensor(
                    svar2[:, :], ssq2[:, :], 1.0 / D, svar2[:, :],
                    op0=ALU.mult, op1=ALU.subtract),
                    waits=[tok_q2, ve], inc=ve)
                tok_v2 = ve.tok()
                pr.emit("se", lambda e: e.activation(
                    srstd2[:, :], svar2[:, :], AF.Sqrt, bias=eps_col[:, 0:1]),
                    waits=[tok_v2, se], inc=se)
                tok_sd2 = se.tok()
                pr.emit("ve", lambda e: e.reciprocal(srstd2[:, :], srstd2[:, :]),
                        waits=[tok_sd2, ve], inc=ve)
                pr.emit("ve", lambda e: e.tensor_scalar(
                    su2[:, :], smean2[:, :], srstd2[:, 0:1], -1.0,
                    op0=ALU.mult, op1=ALU.mult),
                    waits=[ve], inc=ve)
                pr.emit("ve", lambda e: e.tensor_scalar(
                    outr[:, :], sful2[:, :], srstd2[:, 0:1], su2[:, 0:1],
                    op0=ALU.mult, op1=ALU.add),
                    waits=[ve, (sy_o, sy_o.n)], inc=ve)
                tok_out = ve.tok()
                pr.emit("sy", lambda e, tb=tb: e.dma_start(
                    out=out_d.ap()[tb * 128:(tb + 1) * 128, :], in_=outr[:, :]),
                    waits=[tok_out], inc=sy_o)
            p65p.close()
            p65.close()

        barrier_all()
        for tname in C.taps:
            src = ds[tname]
            dst = tap_outs[tname]
            nrow = src.shape[0]
            for r0 in range(0, nrow, 2048):
                r1 = min(nrow, r0 + 2048)
                pr.emit("sy", lambda e, src=src, dst=dst, r0=r0, r1=r1: e.dma_start(
                    out=dst.ap()[r0:r1, :], in_=src.ap()[r0:r1, :]), inc=sy_o)
        final = pr.snapshot()
        pr.emit("sy", lambda e: None, waits=final)
        pr.emit("ge", lambda e: None, waits=[(cc, 1 if cc.n else 0)])

        with nc.Block() as block:
            @block.sync
            def _(e):
                for f in pr.q["sy"]:
                    f(e)
            @block.gpsimd
            def _(e):
                for f in pr.q["ge"]:
                    f(e)
            @block.tensor
            def _(e):
                for f in pr.q["te"]:
                    f(e)
            @block.vector
            def _(e):
                for f in pr.q["ve"]:
                    f(e)
            @block.scalar
            def _(e):
                for f in pr.q["se"]:
                    f(e)
    return nc


def _stage(inputs):
    x = np.asarray(inputs["L0_attn_x"], dtype=np.float32)
    win = np.asarray(inputs["L0_attn_in_proj_w"], dtype=np.float32)
    wout = np.asarray(inputs["L0_attn_out_proj_w"], dtype=np.float32)
    wbase = np.asarray(inputs["L0_ffn_W_base"], dtype=np.float32)

    def pretile(wt, kdim, odim):
        kc, oc = kdim // 128, odim // 128
        return np.ascontiguousarray(
            wt.reshape(kc, 128, oc, 128).transpose(2, 1, 0, 3).reshape(odim, kdim))

    win_s = pretile(np.ascontiguousarray(win.T), D, PROJD)
    wout_s = pretile(np.ascontiguousarray(wout.T), DI, D)
    wbase_s = pretile(np.ascontiguousarray(wbase.T), D, D)
    q = _fourier_q()
    enc01_s = np.ascontiguousarray(
        np.concatenate([inputs["L0_ffn_enc0"], inputs["L0_ffn_enc1"]], axis=0)
        .T.reshape(NDC, 128, R01).transpose(1, 0, 2).reshape(128, NDC * R01)
    ).astype(np.float32)
    def ptile(a, w):
        return np.ascontiguousarray(
            np.asarray(a).transpose(1, 0, 2).reshape(D, w)
            .reshape(NDC, 128, w).transpose(1, 0, 2).reshape(128, NDC * w))
    c0_s = ptile(inputs["L0_ffn_coefs0"], 96).astype(np.float32)
    m0_s = ptile(inputs["L0_ffn_mask0"], 96).astype(np.uint8)
    c1_s = ptile(inputs["L0_ffn_coefs1"], 24).astype(np.float32)
    m1_s = ptile(inputs["L0_ffn_mask1"], 24).astype(np.uint8)
    convw_r = np.ascontiguousarray(
        np.asarray(inputs["L0_attn_conv_w"], dtype=np.float32)
        .reshape(80, 128, 4).transpose(1, 0, 2).reshape(128, 320))
    convb_r = np.ascontiguousarray(
        np.asarray(inputs["L0_attn_conv_b"], dtype=np.float32)
        .reshape(80, 128).T)
    dtb_c = np.asarray(inputs["L0_attn_dt_bias"], dtype=np.float32).reshape(128, 1)
    alog_c = np.asarray(inputs["L0_attn_A_log"], dtype=np.float32).reshape(128, 1)
    dp_r = np.asarray(inputs["L0_attn_Dp"], dtype=np.float32).reshape(1, 128)
    normw_r = np.ascontiguousarray(
        np.asarray(inputs["L0_attn_norm_w"], dtype=np.float32).reshape(NDC, 128).T)
    gw_r = np.asarray(inputs["L0_attn_gnorm_w"], dtype=np.float32).reshape(1, DI)
    bbase_r = np.ascontiguousarray(
        np.asarray(inputs["L0_ffn_b_base"], dtype=np.float32).reshape(NDC, 128).T)
    ident_b = np.eye(128, dtype=np.float32).astype(ml_dtypes.bfloat16)
    ident_f = np.eye(128, dtype=np.float32)
    ones_f = np.ones((1, 128), dtype=np.float32)
    ones_b = np.ones((1, 128), dtype=np.float32).astype(ml_dtypes.bfloat16)
    trimask = (np.triu(np.ones((128, 128), dtype=np.float32))
               ).astype(ml_dtypes.bfloat16)

    shared = dict(
        win_s=win_s, wout_s=wout_s, wbase_s=wbase_s, enc01_s=enc01_s,
        c0_s=c0_s, m0_s=m0_s, c1_s=c1_s, m1_s=m1_s,
        convw_r=convw_r, convb_r=convb_r, dtb_c=dtb_c, alog_c=alog_c,
        dp_r=dp_r, normw_r=normw_r, gw_r=gw_r, bbase_r=bbase_r,
        ident_b=ident_b, ident_f=ident_f, ones_f=ones_f, ones_b=ones_b,
        trimask=trimask,
    )
    in_maps = []
    for k in range(8):
        b = k // 2
        half = k % 2
        xs = np.zeros((TX, D), dtype=np.float32)
        if half == 1:
            xs[LEAD - 3:LEAD, :] = x[b, T - 3:T, :]
        xs[LEAD:, :] = x[b, half * T:(half + 1) * T, :]
        q_half = np.ascontiguousarray(q[:, half * T:(half + 1) * T])
        par = np.full((128, 1), float(half), dtype=np.float32)
        m = dict(shared)
        m["x_in"] = xs
        m["q_half"] = q_half
        m["parity"] = par
        in_maps.append(m)
    return in_maps


def kernel(**inputs):
    in_maps = _stage(inputs)
    nc = build_nc(taps=DEBUG_TAPS)
    res = run_bass_kernel_spmd(nc, in_maps, core_ids=list(range(8)))
    out = np.empty((B, SQ, D), dtype=np.float32)
    for k in range(8):
        b, half = k // 2, k % 2
        out[b, half * T:(half + 1) * T, :] = res.results[k]["out"]
    if DEBUG_TAPS:
        kernel.debug = res
    return out
